# version 06
# baseline (speedup 1.0000x reference)
import numpy as np
import ml_dtypes
import concourse.bass as bass
import concourse.mybir as mybir
from concourse.bass_utils import run_bass_kernel_spmd
from contextlib import ExitStack

F32 = mybir.dt.float32
BF16 = mybir.dt.bfloat16
F16 = mybir.dt.float16
I8 = mybir.dt.int8
AF = mybir.ActivationFunctionType
ALU = mybir.AluOpType
AX = mybir.AxisListType

NCORES = 8
HD = 128
MAGIC = 12582912.0
CFG = dict(D=4096, L=4096, B=2, NH=32, TT=512)

ENGINES = ["tensor", "vector", "scalar", "gpsimd", "sync"]


def derive(cfg):
    d = dict(cfg)
    D, L, B, NH, TT = d["D"], d["L"], d["B"], d["NH"], d["TT"]
    d["TOK"] = B * L
    d["TPC"] = d["TOK"] // NCORES
    d["OSL"] = D // NCORES
    d["HPC"] = NH // NCORES
    d["DC"] = D // 128
    d["OC"] = d["OSL"] // 128
    d["NTT"] = d["TOK"] // TT
    d["NTT1"] = d["TPC"] // TT
    d["NQT"] = L // TT
    d["KCB"] = L // 128
    d["ND"] = TT // 128
    d["VSTR"] = 144
    assert d["TPC"] % TT == 0 and d["OSL"] % 128 == 0 and L % TT == 0
    return d


class Prog:

    def __init__(self):
        self.ops = {e: [] for e in ENGINES}
        self.cnt = {}

    def c(self, s):
        return self.cnt.get(s, 0)

    def add(self, eng, fn, waits=(), inc=None, k=1):
        w = [(s, v) for (s, v) in waits if v is not None and v > 0]
        self.ops[eng].append((fn, w, inc, k))
        if inc is not None:
            self.cnt[inc] = self.cnt.get(inc, 0) + k
            return self.cnt[inc]
        return None

    def barrier(self):
        qsem = {"scalar": "dA", "sync": "dS", "gpsimd": "dG"}
        self.cnt["pb"] = self.c("pb") + 5
        tgt = self.c("pb")
        for e in ENGINES:
            w = [("pb", tgt)]
            if e in qsem and self.c(qsem[e]) > 0:
                w.append((qsem[e], self.c(qsem[e])))
            self.ops[e].append((None, w, "pb_inc", 1))


def build(cfg, dbg=False):
    c = derive(cfg)
    D, L, B, TT = c["D"], c["L"], c["B"], c["TT"]
    TOK, TPC, OSL, HPC = c["TOK"], c["TPC"], c["OSL"], c["HPC"]
    DC, OC, NTT, NTT1, NQT, ND, VSTR, KCB = (
        c["DC"], c["OC"], c["NTT"], c["NTT1"], c["NQT"], c["ND"], c["VSTR"], c["KCB"])
    SC = 1.0 / float(np.sqrt(np.float64(HD)))

    nc = bass.Bass("TRN2", target_bir_lowering=False, detect_race_conditions=False)

    xt_tok = nc.dram_tensor("xt_tok", [D, TPC], F32, kind="ExternalInput").ap()
    xt_res = nc.dram_tensor("xt_res", [OSL, TOK], F32, kind="ExternalInput").ap()
    wT = {w: nc.dram_tensor(f"w{w}T", [D, OSL], F32, kind="ExternalInput").ap()
          for w in "qkvo"}
    bqv = nc.dram_tensor("bqv", [128, 3 * OC], F32, kind="ExternalInput").ap()
    bo_in = nc.dram_tensor("bo_in", [128, OC], F32, kind="ExternalInput").ap()
    fb_in = nc.dram_tensor("fb_in", [128, OC], F16, kind="ExternalInput").ap()
    fsc_in = nc.dram_tensor("fsc_in", [1, 1], F16, kind="ExternalInput").ap()
    fpk = nc.dram_tensor("fpk", [D // 2, OSL], I8, kind="ExternalInput").ap()
    masks = nc.dram_tensor("masks", [128, ND * TT], BF16, kind="ExternalInput").ap()
    ident_in = nc.dram_tensor("ident_in", [128, 128], BF16, kind="ExternalInput").ap()

    dk = dict(kind="ExternalOutput") if dbg else {}
    xnT_in = [nc.dram_tensor(f"xnT_in{t}", [D, TPC // NTT1], BF16) for t in range(NTT1)]
    xnT_all = [nc.dram_tensor(f"xnT_all{t}", [NCORES * D, TPC // NTT1], BF16,
                              addr_space="Shared") for t in range(NTT1)]
    qT_d = nc.dram_tensor("qT_d", [OSL, TOK], BF16, **dk).ap()
    kT_d = nc.dram_tensor("kT_d", [OSL, TOK], BF16, **dk).ap()
    vT_d = nc.dram_tensor("vT_d", [OSL, TOK], BF16, **dk).ap()
    attT_in = nc.dram_tensor("attT_in", [OSL, TOK], BF16)
    attT_all = nc.dram_tensor("attT_all", [NCORES * OSL, TOK], BF16, addr_space="Shared")
    attT_dbg = (nc.dram_tensor("attT_dbg", [OSL, TOK], BF16, kind="ExternalOutput")
                if dbg else None)
    outT = nc.dram_tensor("outT", [OSL, TOK], F32, kind="ExternalOutput").ap()
    xnT_in_ap = [t.ap() for t in xnT_in]
    xnT_all_ap = [t.ap() for t in xnT_all]
    attT_in_ap, attT_all_ap = attT_in.ap(), attT_all.ap()

    es = ExitStack()
    sems = {n: es.enter_context(nc.semaphore(n))
            for n in ["dG", "dS", "dA", "cT", "cV", "cS", "cc1", "cc2", "pb"]}

    psA = es.enter_context(nc.psum_tensor("psA", [128, 1024], F32))
    psB = es.enter_context(nc.psum_tensor("psB", [128, 1024], F32))
    pav = [es.enter_context(nc.psum_tensor(f"pav{i}", [128, 512], F32)) for i in range(2)]
    ptr = [es.enter_context(nc.psum_tensor(f"ptr{i}", [128, 128], BF16)) for i in range(2)]
    PS4 = [psA[:, 0:TT], psB[:, 0:TT], pav[0][:, 0:TT], pav[1][:, 0:TT]]

    sbM = es.enter_context(nc.sbuf_tensor("sbM", [128, ND * TT], BF16))
    sbI = es.enter_context(nc.sbuf_tensor("sbI", [128, 128], BF16))
    sb1cb = es.enter_context(nc.sbuf_tensor("sb1cb", [128, 1], BF16))
    sb1cf = es.enter_context(nc.sbuf_tensor("sb1cf", [128, 1], F32))
    sb1rf = es.enter_context(nc.sbuf_tensor("sb1rf", [1, 128], F32))
    sbBQV = es.enter_context(nc.sbuf_tensor("sbBQV", [128, 3 * OC], F32))
    sbBO = es.enter_context(nc.sbuf_tensor("sbBO", [128, OC], F32))
    sbFB = es.enter_context(nc.sbuf_tensor("sbFB", [128, OC], F32))
    sbFBh = es.enter_context(nc.sbuf_tensor("sbFBh", [128, OC], F16))
    sbFS = es.enter_context(nc.sbuf_tensor("sbFS", [128, 1], F32))
    sbFSh = es.enter_context(nc.sbuf_tensor("sbFSh", [1, 1], F16))
    sbFSf = es.enter_context(nc.sbuf_tensor("sbFSf", [1, 1], F32))
    sbEPS = es.enter_context(nc.sbuf_tensor("sbEPS", [1, 1], F32))
    rb = [es.enter_context(nc.sbuf_tensor(f"rb{i}", [128, DC * TT], BF16)) for i in range(2)]
    stage = [es.enter_context(nc.sbuf_tensor(f"stg{i}", [128, TT], BF16)) for i in range(4)]

    P = Prog()

    def load_fm(eng, dst_sb, src_ap, row0, col0, tt, nsplit, sm, waits=(), nch=None):
        dcx = nch if nch is not None else dst_sb.shape[1] // tt
        v3 = src_ap[row0:row0 + dcx * 128, col0:col0 + tt].rearrange("(c p) t -> p c t", p=128)
        d3 = dst_sb[:, 0:dcx * tt].rearrange("p (c t) -> p c t", t=tt)
        step = dcx // nsplit
        last = None
        for s in range(nsplit):
            w = waits if s == 0 else ()
            last = P.add(eng, lambda e, o=d3[:, s * step:(s + 1) * step, :],
                         i=v3[:, s * step:(s + 1) * step, :]: e.dma_start(out=o, in_=i),
                         waits=w, inc=sm, k=16)
        return last

    for dst, src in [(sbM, masks), (sbI, ident_in), (sbBQV, bqv), (sbBO, bo_in),
                     (sbFBh, fb_in), (sbFSh, fsc_in)]:
        P.add("gpsimd", lambda e, d=dst, s=src: e.dma_start(out=d[:, :], in_=s[:, :]),
              inc="dG", k=16)
    P.add("gpsimd", lambda e: e.memset(sb1cb[:, :], 1.0))
    P.add("gpsimd", lambda e: e.memset(sb1cf[:, :], 1.0))
    P.add("gpsimd", lambda e: e.memset(sb1rf[:, :], 1.0))
    m_consts = P.add("gpsimd", lambda e: e.memset(sbEPS[:, :], 1e-5), inc="cV")
    d_const = P.c("dG")
    m_fb = P.add("vector", lambda e: e.tensor_copy(sbFB[:, :], sbFBh[:, :]),
                 waits=[("dG", d_const)], inc="cV")
    m_fsf = P.add("vector", lambda e: e.tensor_copy(sbFSf[:, :], sbFSh[:, :]), inc="cV")

    p1s = ExitStack()
    sbx = p1s.enter_context(nc.sbuf_tensor("sbx", [128, DC * TT], F32))
    sbxn = p1s.enter_context(nc.sbuf_tensor("sbxn", [128, DC * TT], BF16))
    sbsq = [p1s.enter_context(nc.sbuf_tensor(f"sbsq{i}", [128, TT], F32)) for i in range(2)]
    sbrow = p1s.enter_context(nc.sbuf_tensor("sbrow", [1, 12 * TT], F32))

    mxn_prev, mst_prev, mvar_prev = None, None, None
    for tt in range(NTT1):
        ROF = (tt % 2) * 6 * TT
        R_MEAN, R_EX2, R_VAR, R_RSTD, R_NMR, R_SCR = (
            slice(ROF, ROF + TT), slice(ROF + TT, ROF + 2 * TT),
            slice(ROF + 2 * TT, ROF + 3 * TT), slice(ROF + 3 * TT, ROF + 4 * TT),
            slice(ROF + 4 * TT, ROF + 5 * TT), slice(ROF + 5 * TT, ROF + 6 * TT))
        lw = [("cV", mxn_prev)] if tt > 0 else [("cV", m_consts)]
        mload = load_fm("sync", sbx, xt_tok, 0, tt * TT, TT, 4, "dS", waits=lw)
        mA = None
        for i in range(DC):
            w = []
            if i == 0:
                w.append(("dS", mload))
                if mvar_prev is not None:
                    w.append(("cV", mvar_prev))
            mA = P.add("tensor", lambda e, i=i: e.matmul(
                pav[0][0:1, 0:TT], sb1cf[:, :], sbx[:, i * TT:(i + 1) * TT],
                start=(i == 0), stop=(i == DC - 1)), waits=w,
                inc=("cT" if i == DC - 1 else None))
        sqt = {}
        mB = None
        for i in range(DC):
            w = []
            if i == 0:
                w.append(("dS", mload))
            if i >= 2:
                w.append(("cT", sqt[i - 2]))
            msq = P.add("vector", lambda e, i=i: e.tensor_mul(
                sbsq[i % 2][:, :], sbx[:, i * TT:(i + 1) * TT], sbx[:, i * TT:(i + 1) * TT]),
                waits=w, inc="cV")
            sqt[i] = P.add("tensor", lambda e, i=i: e.matmul(
                pav[1][0:1, 0:TT], sb1cf[:, :], sbsq[i % 2][:, :],
                start=(i == 0), stop=(i == DC - 1)),
                waits=[("cV", msq)], inc="cT")
            mB = sqt[i]
        inv = 1.0 / D
        P.add("vector", lambda e, R_MEAN=R_MEAN: e.tensor_single_scalar(
            sbrow[0:1, R_MEAN], pav[0][0:1, 0:TT], inv, ALU.mult), waits=[("cT", mA)])
        P.add("vector", lambda e, R_EX2=R_EX2: e.tensor_single_scalar(
            sbrow[0:1, R_EX2], pav[1][0:1, 0:TT], inv, ALU.mult), waits=[("cT", mB)])
        m_msq = P.add("vector", lambda e, R_SCR=R_SCR, R_MEAN=R_MEAN: e.tensor_mul(
            sbrow[0:1, R_SCR], sbrow[0:1, R_MEAN], sbrow[0:1, R_MEAN]), inc="cV")
        m_var = P.add("vector", lambda e, R_VAR=R_VAR, R_EX2=R_EX2, R_SCR=R_SCR: e.tensor_sub(
            sbrow[0:1, R_VAR], sbrow[0:1, R_EX2], sbrow[0:1, R_SCR]),
            waits=[("cV", m_msq)], inc="cV")
        mvar_prev = m_var
        m_ln1 = P.add("scalar", lambda e, R_SCR=R_SCR, R_VAR=R_VAR: e.activation(
            sbrow[0:1, R_SCR], sbrow[0:1, R_VAR], AF.Ln, bias=sbEPS[0:1, 0:1], scale=1.0),
            waits=[("cV", m_var)], inc="cS")
        m_rstd = P.add("scalar", lambda e, R_RSTD=R_RSTD, R_SCR=R_SCR: e.activation(
            sbrow[0:1, R_RSTD], sbrow[0:1, R_SCR], AF.Exp, scale=-0.5),
            waits=[("cS", m_ln1)], inc="cS")
        m_nmr = P.add("vector", lambda e, R_NMR=R_NMR, R_MEAN=R_MEAN, R_RSTD=R_RSTD:
                      e.scalar_tensor_tensor(sbrow[0:1, R_NMR], sbrow[0:1, R_MEAN], -1.0,
                                             sbrow[0:1, R_RSTD], ALU.mult, ALU.mult),
                      waits=[("cS", m_rstd)], inc="cV")
        wb = [("cS", m_rstd)]
        if mxn_prev is not None:
            wb.append(("cV", mxn_prev))
        m_bA = P.add("tensor", lambda e, R_RSTD=R_RSTD: e.matmul(
            psA[:, 0:TT], sb1rf[:, :], sbrow[0:1, R_RSTD], start=True, stop=True),
            waits=wb, inc="cT")
        m_bB = P.add("tensor", lambda e, R_NMR=R_NMR: e.matmul(
            psB[:, 0:TT], sb1rf[:, :], sbrow[0:1, R_NMR], start=True, stop=True),
            waits=[("cV", m_nmr)], inc="cT")
        mxn = None
        for i in range(DC):
            w = [("cT", m_bB)] if i == 0 else []
            if i == 0 and mst_prev is not None:
                w.append(("dG", mst_prev))
            P.add("vector", lambda e, i=i: e.tensor_mul(
                sbsq[i % 2][:, :], sbx[:, i * TT:(i + 1) * TT], psA[:, 0:TT]), waits=w)
            mxn = P.add("vector", lambda e, i=i: e.tensor_add(
                sbxn[:, i * TT:(i + 1) * TT], sbsq[i % 2][:, :], psB[:, 0:TT]),
                inc=("cV" if i == DC - 1 else None))
        mxn_prev = mxn
        x3 = xnT_in_ap[tt][:, :].rearrange("(c p) t -> p c t", p=128)
        s3 = sbxn[:, :].rearrange("p (c t) -> p c t", t=TT)
        st = DC // 4
        for s in range(4):
            w = [("cV", mxn)] if s == 0 else ()
            mst_prev = P.add("gpsimd", lambda e, s=s, x3=x3, s3=s3: e.dma_start(
                out=x3[:, s * st:(s + 1) * st, :], in_=s3[:, s * st:(s + 1) * st, :]),
                waits=w, inc="dG", k=16)
        P.add("gpsimd", lambda e, tt=tt: e.collective_compute(
            "AllGather", ALU.bypass, replica_groups=[list(range(NCORES))],
            ins=[xnT_in_ap[tt].opt()], outs=[xnT_all_ap[tt].opt()]),
            waits=[("dG", P.c("dG"))], inc="cc1", k=1)

    P.barrier()
    p1s.close()

    p2s = ExitStack()
    sbW = {w: p2s.enter_context(nc.sbuf_tensor(f"sbW{w}", [128, DC * OSL], BF16))
           for w in "qkv"}
    for w in "qkv":
        load_fm("gpsimd", sbW[w], wT[w], 0, 0, OSL, 4, "dG")
    m_wload = P.c("dG")

    NW = 3 * HPC
    NTILE = NCORES * NTT1
    tile_done = []
    grp_evac = []
    grp_store = []
    for k in range(NTILE):
        tt, p = divmod(k, NCORES)
        w0 = [("cc1", tt + 1)] if p == 0 else []
        if k >= 2:
            w0.append(("cT", tile_done[k - 2]))
        mload = load_fm("sync", rb[k % 2], xnT_all_ap[tt], p * D, 0, TT, 4, "dS", waits=w0)
        mm = None
        for wh in range(NW):
            g = k * NW + wh
            w = "qkv"[wh // HPC]
            h = wh % HPC
            ps = PS4[g % 4]
            waits = []
            if wh == 0:
                waits.append(("dS", mload))
                if k == 0:
                    waits.append(("dG", m_wload))
            if g >= 4:
                waits.append(("cV", grp_evac[g - 4]))
            for i in range(DC):
                mm = P.add("tensor", lambda e, i=i, w=w, h=h, ps=ps, k=k: e.matmul(
                    ps, sbW[w][:, i * OSL + h * 128: i * OSL + h * 128 + 128],
                    rb[k % 2][:, i * TT:(i + 1) * TT],
                    start=(i == 0), stop=(i == DC - 1)),
                    waits=(waits if i == 0 else ()),
                    inc=("cT" if i == DC - 1 else None))
            bcol = "qkv".index(w) * OC + h
            ev_w = [("cT", mm)]
            if g >= 4:
                ev_w.append(("dA", grp_store[g - 4]))
            me = P.add("vector", lambda e, g=g, ps=ps, bcol=bcol: e.tensor_single_scalar(
                stage[g % 4][:, :], ps, sbBQV[:, bcol:bcol + 1], ALU.add),
                waits=ev_w, inc="cV")
            grp_evac.append(me)
            dst = {"q": qT_d, "k": kT_d, "v": vT_d}[w]
            ms = P.add("scalar", lambda e, g=g, dst=dst, h=h, p=p, tt=tt: e.dma_start(
                out=dst[h * 128:(h + 1) * 128, p * TPC + tt * TT: p * TPC + tt * TT + TT],
                in_=stage[g % 4][:, :]), waits=[("cV", me)], inc="dA", k=16)
            grp_store.append(ms)
        tile_done.append(mm)

    P.barrier()
    p2s.close()

    p3s = ExitStack()
    sbQ = [p3s.enter_context(nc.sbuf_tensor(f"sbQ{i}", [128, L], BF16)) for i in range(2)]
    sbK = [p3s.enter_context(nc.sbuf_tensor(f"sbK{i}", [128, L], BF16)) for i in range(2)]
    sbV = p3s.enter_context(nc.sbuf_tensor("sbV", [128, KCB * HPC * VSTR], BF16))
    vstg = [p3s.enter_context(nc.sbuf_tensor(f"vstg{i}", [128, OSL], BF16)) for i in range(4)]
    sbP = p3s.enter_context(nc.sbuf_tensor("sbP", [128, KCB * TT], BF16))
    sbAT = [p3s.enter_context(nc.sbuf_tensor(f"sbAT{i}", [128, L], BF16)) for i in range(2)]
    sbSQ = p3s.enter_context(nc.sbuf_tensor("sbSQ", [128, L], BF16))
    sbNB = p3s.enter_context(nc.sbuf_tensor("sbNB", [1, 8], F32))
    sbnb = p3s.enter_context(nc.sbuf_tensor("sbnb", [128, 1], F32))
    sbRC = p3s.enter_context(nc.sbuf_tensor("sbRC", [128, 1], F32))
    sbA = [p3s.enter_context(nc.sbuf_tensor(f"sbA{i}", [128, 128], BF16)) for i in range(2)]

    head_done_t = {}
    qk_vrd = {}
    at_store = {}
    exp_slot = {0: None, 1: None}
    pav_rd = {0: None, 1: None}
    sbA_rd = {0: None, 1: None}
    ptr_rd = {0: None, 1: None}
    sbSQ_rd = None
    prev_av_done = None
    prev_av_kc = []
    nl = L // TT

    for b in range(B):
        vw = []
        if b > 0:
            vw.append(("cT", head_done_t[b * HPC - 1]))
        m_vms = P.add("vector", lambda e: e.memset(sbV[:, :], 1.0), waits=vw, inc="cV")
        vcp = []
        for kc in range(KCB):
            w = []
            if len(vcp) >= 4:
                w.append(("cV", vcp[-4]))
            mt = P.add("sync", lambda e, kc=kc, b=b: e.dma_start(
                out=vstg[kc % 4][:, :],
                in_=vT_d[0:OSL, b * L + kc * 128: b * L + kc * 128 + 128],
                transpose=True), waits=w, inc="dS", k=16)
            o3 = sbV[:, kc * HPC * VSTR:(kc + 1) * HPC * VSTR].rearrange(
                "p (h s) -> p h s", s=VSTR)[:, :, 0:128]
            i3 = vstg[kc % 4][:, :].rearrange("p (h s) -> p h s", s=128)
            wv = [("dS", mt)]
            if kc == 0:
                wv.append(("cV", m_vms))
            vcp.append(P.add("vector", lambda e, o3=o3, i3=i3: e.tensor_copy(o3, i3),
                             waits=wv, inc="cV"))
        m_vdone = vcp[-1]

        for h in range(HPC):
            bh = b * HPC + h
            prev_av_kc = []
            wqk = []
            if bh >= 2:
                wqk = [("cT", head_done_t[bh - 2]), ("cV", qk_vrd[bh - 2])]
            mq = P.add("sync", lambda e, h=h, b=b, bh=bh: e.dma_start(
                out=sbQ[bh % 2][:, :], in_=qT_d[h * 128:(h + 1) * 128, b * L:(b + 1) * L]),
                waits=wqk, inc="dS", k=16)
            mk = P.add("sync", lambda e, h=h, b=b, bh=bh: e.dma_start(
                out=sbK[bh % 2][:, :], in_=kT_d[h * 128:(h + 1) * 128, b * L:(b + 1) * L]),
                inc="dS", k=16)

            def side_reduce(src_mile, out_col):
                nonlocal sbSQ_rd
                mred = None
                for j in range(nl):
                    sl = pav[j % 2]
                    w = [("cV", src_mile)] if j == 0 else []
                    w.append(("cV", pav_rd[j % 2]))
                    mmj = P.add("tensor", lambda e, j=j, sl=sl: e.matmul(
                        sl[0:1, 0:TT], sb1cb[:, :], sbSQ[:, j * TT:(j + 1) * TT],
                        start=True, stop=True), waits=w, inc="cT")
                    sbSQ_rd = mmj
                    mrd = P.add("vector", lambda e, sl=sl: e.tensor_reduce(
                        sbNB[0:1, 6:7], sl[0:1, 0:TT], AX.X, ALU.max),
                        waits=[("cT", mmj)], inc="cV")
                    if j == 0:
                        mred = P.add("vector", lambda e, oc=out_col: e.tensor_copy(
                            sbNB[0:1, oc:oc + 1], sbNB[0:1, 6:7]),
                            waits=[("cV", mrd)], inc="cV")
                    else:
                        mred = P.add("vector", lambda e, oc=out_col: e.tensor_max(
                            sbNB[0:1, oc:oc + 1], sbNB[0:1, oc:oc + 1], sbNB[0:1, 6:7]),
                            waits=[("cV", mrd)], inc="cV")
                    pav_rd[j % 2] = mred
                return mred

            wsq = [("dS", mq), ("cT", sbSQ_rd)]
            m_sq = P.add("vector", lambda e, bh=bh: e.tensor_mul(
                sbSQ[:, :], sbQ[bh % 2][:, :], sbQ[bh % 2][:, :]), waits=wsq, inc="cV")
            side_reduce(m_sq, 0)
            m_sqk = P.add("vector", lambda e, bh=bh: e.tensor_mul(
                sbSQ[:, :], sbK[bh % 2][:, :], sbK[bh % 2][:, :]),
                waits=[("dS", mk), ("cT", sbSQ_rd)], inc="cV")
            mrk = side_reduce(m_sqk, 1)
            qk_vrd[bh] = mrk
            m_pr = P.add("vector", lambda e: e.tensor_mul(
                sbNB[0:1, 2:3], sbNB[0:1, 0:1], sbNB[0:1, 1:2]),
                waits=[("cV", mrk)], inc="cV")
            m_ln = P.add("scalar", lambda e: e.activation(sbNB[0:1, 3:4], sbNB[0:1, 2:3],
                                                          AF.Ln, scale=1.0),
                         waits=[("cV", m_pr)], inc="cS")
            m_nbs = P.add("scalar", lambda e: e.activation(
                sbNB[0:1, 5:6], sbNB[0:1, 3:4], AF.Exp, scale=0.5),
                waits=[("cS", m_ln)], inc="cS")
            m_nbv = P.add("vector", lambda e: e.tensor_single_scalar(
                sbNB[0:1, 5:6], sbNB[0:1, 5:6], -SC, ALU.mult),
                waits=[("cS", m_nbs)], inc="cV")
            m_nbb = P.add("tensor", lambda e: e.matmul(
                pav[0][:, 0:1], sb1rf[:, :], sbNB[0:1, 5:6], start=True, stop=True),
                waits=[("cV", m_nbv), ("cV", pav_rd[0])], inc="cT")
            m_nb = P.add("vector", lambda e: e.tensor_copy(sbnb[:, :], pav[0][:, 0:1]),
                         waits=[("cT", m_nbb)], inc="cV")
            pav_rd[0] = m_nb

            for qt in range(NQT):
                nck = (qt + 1) * ND
                npairs = (nck + 1) // 2
                exp_m = []
                for pr in range(npairs):
                    kc0 = pr * 2
                    np_ = min(2, nck - kc0)
                    slot = pr % 2
                    ps = psA if slot == 0 else psB
                    w = [("cS", exp_slot[slot])]
                    mm = None
                    for kk in range(np_):
                        kc = kc0 + kk
                        mm = P.add("tensor", lambda e, kk=kk, kc=kc, ps=ps, bh=bh, qt=qt: e.matmul(
                            ps[:, kk * TT:(kk + 1) * TT],
                            sbK[bh % 2][:, kc * 128:(kc + 1) * 128],
                            sbQ[bh % 2][:, qt * TT:(qt + 1) * TT],
                            start=True, stop=True),
                            waits=(w if kk == 0 else ()),
                            inc=("cT" if kk == np_ - 1 else None))
                    ew = [("cT", mm), ("cV", m_nb), ("cT", prev_av_done)]
                    if qt > 0 and prev_av_kc:
                        ew.append(("cT", prev_av_kc[min(kc0 + np_ - 1, len(prev_av_kc) - 1)]))
                    if pr == 0:
                        ew.append(("cV", m_vdone))
                    me = P.add("scalar", lambda e, kc0=kc0, np_=np_, ps=ps: e.activation(
                        sbP[:, kc0 * TT:(kc0 + np_) * TT], ps[:, 0:np_ * TT],
                        AF.Exp, bias=sbnb[:, :], scale=SC), waits=ew, inc="cS")
                    exp_m.append(me)
                    exp_slot[slot] = me
                msk_m = {}
                for j in range(ND):
                    kc = nck - ND + j
                    msk_m[kc] = P.add("vector", lambda e, kc=kc, j=j: e.tensor_mul(
                        sbP[:, kc * TT:(kc + 1) * TT], sbP[:, kc * TT:(kc + 1) * TT],
                        sbM[:, j * TT:(j + 1) * TT]),
                        waits=[("cS", exp_m[kc // 2])], inc="cV")
                def av_pass(qcs, first_w):
                    mile = []
                    for kc in range(nck):
                        w = [("cS", exp_m[kc // 2])]
                        if kc in msk_m:
                            w.append(("cV", msk_m[kc]))
                        if kc == 0:
                            w += first_w
                        mmav = None
                        for i, qc in enumerate(qcs):
                            mmav = P.add(
                                "tensor", lambda e, kc=kc, qc=qc, i=i, h=h, nck=nck: e.matmul(
                                    pav[i][:, 0:129],
                                    sbP[:, kc * TT + qc * 128: kc * TT + qc * 128 + 128],
                                    sbV[:, kc * HPC * VSTR + h * VSTR:
                                        kc * HPC * VSTR + h * VSTR + 129],
                                    start=(kc == 0), stop=(kc == nck - 1)),
                                waits=(w if i == 0 else ()),
                                inc=("cT" if i == len(qcs) - 1 else None))
                        mile.append(mmav)
                    return mile

                def av_norm(qcs, mile_last):
                    out = []
                    for i, qc in enumerate(qcs):
                        pa = pav[i]
                        mrc = P.add("vector", lambda e, pa=pa: e.reciprocal(
                            sbRC[:, :], pa[:, 128:129]),
                            waits=[("cT", mile_last)], inc="cV")
                        mn = P.add("vector", lambda e, pa=pa, qc=qc: e.tensor_single_scalar(
                            sbA[qc % 2][:, :], pa[:, 0:128], sbRC[:, :], ALU.mult),
                            waits=[("cT", sbA_rd[qc % 2]), ("cV", mrc)], inc="cV")
                        out.append(mn)
                        mtr = P.add("tensor", lambda e, qc=qc: e.transpose(
                            ptr[qc % 2][:, :], sbA[qc % 2][:, :], sbI[:, :]),
                            waits=[("cV", mn), ("cV", ptr_rd[qc % 2])], inc="cT")
                        sbA_rd[qc % 2] = mtr
                        wcp = [("cT", mtr)]
                        if qt == 0 and qc == 0 and bh >= 2:
                            wcp.append(("dA", at_store[bh - 2]))
                        mcp = P.add("vector", lambda e, qt=qt, qc=qc, bh=bh: e.tensor_copy(
                            sbAT[bh % 2][:, qt * TT + qc * 128: qt * TT + qc * 128 + 128],
                            ptr[qc % 2][:, :]), waits=wcp, inc="cV")
                        ptr_rd[qc % 2] = mcp
                    return out

                half = (ND + 1) // 2
                qcsA = list(range(0, half))
                qcsB = list(range(half, ND))
                wA = [("cV", pav_rd[0]), ("cV", pav_rd[1])]
                mA = av_pass(qcsA, wA)
                nA = av_norm(qcsA, mA[-1])
                if qcsB:
                    wB = [("cV", nA[-1])]
                    mB = av_pass(qcsB, wB)
                    nB = av_norm(qcsB, mB[-1])
                    pav_rd[0], pav_rd[1] = nB[0], nB[-1]
                    prev_av_kc = mB
                else:
                    pav_rd[0], pav_rd[1] = nA[0], nA[-1]
                    prev_av_kc = mA
                prev_av_done = prev_av_kc[-1]
            head_done_t[bh] = P.c("cT")
            at_store[bh] = P.add("scalar", lambda e, h=h, b=b, bh=bh: e.dma_start(
                out=attT_in_ap[h * 128:(h + 1) * 128, b * L:(b + 1) * L],
                in_=sbAT[bh % 2][:, :]), waits=[("cV", P.c("cV"))], inc="dA", k=16)

    P.add("gpsimd", lambda e: e.collective_compute(
        "AllGather", ALU.bypass, replica_groups=[list(range(NCORES))],
        ins=[attT_in_ap.opt()], outs=[attT_all_ap.opt()]),
        waits=[("dA", P.c("dA"))], inc="cc2", k=1)
    if dbg:
        P.add("gpsimd", lambda e: e.dma_start(out=attT_dbg.ap()[:, :], in_=attT_in_ap[:, :]),
              waits=[("cc2", 1)], inc="dG", k=16)

    P.barrier()
    p3s.close()

    p45 = ExitStack()
    sbWx = p45.enter_context(nc.sbuf_tensor("sbWx", [128, DC * OSL], BF16))
    sbG = p45.enter_context(nc.sbuf_tensor("sbG", [128, OC * TOK], BF16))
    sbU = [p45.enter_context(nc.sbuf_tensor(f"sbU{i}", [128, OSL], F32)) for i in range(3)]
    sbxr = [p45.enter_context(nc.sbuf_tensor(f"sbxr{i}", [128, OC * TT], F32)) for i in range(2)]
    sbSO = p45.enter_context(nc.sbuf_tensor("sbSO", [128, 8 * TT], F32))

    m_fsb = P.add("tensor", lambda e: e.matmul(pav[1][:, 0:1], sb1rf[:, :], sbFSf[:, :],
                                               start=True, stop=True),
                  waits=[("cV", m_fsf)], inc="cT")
    m_fs = P.add("vector", lambda e: e.tensor_copy(sbFS[:, :], pav[1][:, 0:1]),
                 waits=[("cT", m_fsb)], inc="cV")

    mpk = load_fm("gpsimd", sbG, fpk, 0, 0, OSL, 2, "dG", nch=DC // 2)
    m_unp = None
    for j in range(DC // 2):
        sl = slice(j * OSL, (j + 1) * OSL)
        w = [("dG", mpk)] if j == 0 else ()
        mu = P.add("vector", lambda e, sl=sl: e.tensor_copy(sbU[0][:, :], sbG[:, sl]),
                   waits=w, inc="cV")
        mu = P.add("vector", lambda e: e.tensor_scalar(sbU[1][:, :], sbU[0][:, :], 0.0625,
                                                       -0.46875, ALU.mult, ALU.add),
                   waits=[("cV", mu)], inc="cV")
        mu = P.add("vector", lambda e: e.tensor_single_scalar(sbU[1][:, :], sbU[1][:, :], MAGIC, ALU.add),
                   waits=[("cV", mu)], inc="cV")
        mu = P.add("vector", lambda e: e.tensor_single_scalar(sbU[1][:, :], sbU[1][:, :], MAGIC, ALU.subtract),
                   waits=[("cV", mu)], inc="cV")
        P.add("vector", lambda e, j=j: e.tensor_copy(sbWx[:, j * OSL:(j + 1) * OSL], sbU[1][:, :]))
        mu = P.add("vector", lambda e: e.scalar_tensor_tensor(sbU[2][:, :], sbU[1][:, :], -16.0,
                                                              sbU[0][:, :], ALU.mult, ALU.add),
                   waits=[("cV", mu)], inc="cV")
        mu = P.add("vector", lambda e: e.tensor_single_scalar(sbU[0][:, :], sbU[2][:, :], 7.5, ALU.is_gt),
                   waits=[("cV", mu)], inc="cV")
        mu = P.add("vector", lambda e: e.scalar_tensor_tensor(sbU[1][:, :], sbU[0][:, :], -16.0,
                                                              sbU[2][:, :], ALU.mult, ALU.add),
                   waits=[("cV", mu)], inc="cV")
        m_unp = P.add("vector", lambda e, j=j: e.tensor_copy(
            sbWx[:, (DC // 2 + j) * OSL:(DC // 2 + j + 1) * OSL], sbU[1][:, :]),
            waits=[("cV", mu)], inc="cV")

    def mm_phase(addr, epilogue, w_first):
        tile_cons = []
        grp_free = []
        for k in range(NTT):
            src_ap, row0, col0 = addr(k)
            w0 = list(w_first) if k == 0 else []
            if k >= 2:
                w0.append(("cT", tile_cons[k - 2]))
            mload = load_fm("sync", rb[k % 2], src_ap, row0, col0, TT, 4, "dS", waits=w0)
            mm = None
            for oc in range(OC):
                g = k * OC + oc
                ps = PS4[g % 4]
                waits = []
                if oc == 0:
                    waits.append(("dS", mload))
                if len(grp_free) >= 4:
                    waits.append(grp_free[g - 4])
                for i in range(DC):
                    mm = P.add("tensor", lambda e, i=i, oc=oc, ps=ps, k=k: e.matmul(
                        ps, sbWx[:, i * OSL + oc * 128: i * OSL + oc * 128 + 128],
                        rb[k % 2][:, i * TT:(i + 1) * TT],
                        start=(i == 0), stop=(i == DC - 1)),
                        waits=(waits if i == 0 else ()),
                        inc=("cT" if i == DC - 1 else None))
                grp_free.append(epilogue(k, oc, ps, mm))
            tile_cons.append(mm)

    def p4_epi(k, oc, ps, mm):
        m = P.add("scalar", lambda e, k=k, oc=oc, ps=ps: e.activation(
            sbG[:, oc * TOK + k * TT: oc * TOK + k * TT + TT], ps, AF.Silu,
            bias=sbFB[:, oc:oc + 1], scale=sbFS[:, :]),
            waits=[("cT", mm)], inc="cS")
        return ("cS", m)

    mm_phase(lambda k: (xnT_all_ap[k % NTT1], (k // NTT1) * D, 0), p4_epi,
             [("cV", m_unp), ("cV", m_fs), ("cV", m_fb)])

    P.barrier()

    load_fm("gpsimd", sbWx, wT["o"], 0, 0, OSL, 4, "dG")
    m_wo = P.c("dG")
    xr_m = {}
    xr_rd = {0: None, 1: None}

    def load_xr(k):
        w = [("cV", xr_rd[k % 2])] if xr_rd[k % 2] else []
        xr_m[k] = load_fm("gpsimd", sbxr[k % 2], xt_res, 0, k * TT, TT, 1, "dG", waits=w)

    load_xr(0)
    load_xr(1)

    p5_store = []

    def p5_epi(k, oc, ps, mm):
        g = k * OC + oc
        B0 = (g % 2) * 4 * TT
        S_S, S_G, S_T, S_O = (slice(B0, B0 + TT), slice(B0 + TT, B0 + 2 * TT),
                              slice(B0 + 2 * TT, B0 + 3 * TT), slice(B0 + 3 * TT, B0 + 4 * TT))
        mv = P.add("vector", lambda e, ps=ps, oc=oc, S_S=S_S: e.tensor_single_scalar(
            sbSO[:, S_S], ps, sbBO[:, oc:oc + 1], ALU.add),
            waits=[("cT", mm)], inc="cV")
        mv = P.add("vector", lambda e, k=k, oc=oc, S_G=S_G: e.tensor_copy(
            sbSO[:, S_G], sbG[:, oc * TOK + k * TT: oc * TOK + k * TT + TT]),
            waits=[("cV", mv)], inc="cV")
        mv = P.add("vector", lambda e, S_T=S_T, S_S=S_S, S_G=S_G: e.tensor_mul(
            sbSO[:, S_T], sbSO[:, S_S], sbSO[:, S_G]),
            waits=[("cV", mv)], inc="cV")
        wo_ = [("dG", xr_m[k]), ("cV", mv)]
        if g >= 2:
            wo_.append(("dA", p5_store[g - 2]))
        mo = P.add("vector", lambda e, k=k, oc=oc, S_O=S_O, S_T=S_T: e.tensor_add(
            sbSO[:, S_O], sbxr[k % 2][:, oc * TT:(oc + 1) * TT], sbSO[:, S_T]),
            waits=wo_, inc="cV")
        if oc == OC - 1:
            xr_rd[k % 2] = mo
            if k + 2 < NTT:
                load_xr(k + 2)
        p5_store.append(P.add("scalar", lambda e, k=k, oc=oc, S_O=S_O: e.dma_start(
            out=outT[oc * 128:(oc + 1) * 128, k * TT:(k + 1) * TT],
            in_=sbSO[:, S_O]), waits=[("cV", mo)], inc="dA", k=16))
        return ("cV", mo)

    mm_phase(lambda k: (attT_all_ap, 0, k * TT), p5_epi, [("cc2", 1), ("dG", m_wo)])

    P.ops["scalar"].append((None, [("dA", P.c("dA"))], "final_wait", 0))
    p45.close()

    with nc.Block("main") as blk:
        def mk_body(eng):
            def body(e):
                for fn, waits, inc, k in P.ops[eng]:
                    if inc == "pb_inc":
                        e.sem_inc(sems["pb"], 1)
                        for s, v in waits:
                            e.wait_ge(sems[s], v)
                        continue
                    for s, v in waits:
                        e.wait_ge(sems[s], v)
                    if inc == "final_wait":
                        continue
                    ins = fn(e) if fn is not None else None
                    if inc is not None and ins is not None:
                        ins.then_inc(sems[inc], k)
            return body

        blk.tensor(mk_body("tensor"))
        blk.vector(mk_body("vector"))
        blk.scalar(mk_body("scalar"))
        blk.gpsimd(mk_body("gpsimd"))
        blk.sync(mk_body("sync"))

    es.close()
    return nc


def make_inputs(cfg, x, Wq, bq, Wk, bk, Wv, bv, Wo, bo, pk, scale, fbias):
    c = derive(cfg)
    D, L, B, TT = c["D"], c["L"], c["B"], c["TT"]
    TOK, TPC, OSL, OC, ND = c["TOK"], c["TPC"], c["OSL"], c["OC"], c["ND"]
    perm = np.concatenate([np.arange(0, D, 2), np.arange(1, D, 2)])
    xT = np.asarray(x).reshape(TOK, D).T
    xTp = xT[perm]
    pkT = np.asarray(pk).T

    kk = np.arange(128)[:, None]
    qq = np.arange(TT)[None, :]
    mask = np.concatenate([(qq >= kk + 128 * j) for j in range(ND)], axis=1)
    mask = mask.astype(ml_dtypes.bfloat16)
    ident = np.eye(128, dtype=ml_dtypes.bfloat16)

    def bias_chunks(v, n0, n1):
        return np.ascontiguousarray(np.asarray(v)[n0:n1].reshape(-1, 128).T)

    ins = []
    for cid in range(NCORES):
        o0, o1 = cid * OSL, (cid + 1) * OSL
        m = {
            "xt_tok": np.ascontiguousarray(xTp[:, cid * TPC:(cid + 1) * TPC]),
            "xt_res": np.ascontiguousarray(xT[o0:o1, :]),
            "wqT": np.ascontiguousarray(np.asarray(Wq)[o0:o1, :].T[perm]),
            "wkT": np.ascontiguousarray(np.asarray(Wk)[o0:o1, :].T[perm]),
            "wvT": np.ascontiguousarray(np.asarray(Wv)[o0:o1, :].T[perm]),
            "woT": np.ascontiguousarray(np.asarray(Wo)[o0:o1, :].T),
            "bqv": np.ascontiguousarray(np.concatenate(
                [bias_chunks(bq, o0, o1), bias_chunks(bk, o0, o1),
                 bias_chunks(bv, o0, o1)], axis=1)),
            "bo_in": bias_chunks(bo, o0, o1),
            "fb_in": np.ascontiguousarray(np.asarray(fbias)[o0:o1].reshape(-1, 128).T),
            "fsc_in": np.asarray(scale).reshape(1, 1),
            "fpk": np.ascontiguousarray(pkT[:, o0:o1]),
            "masks": mask,
            "ident_in": ident,
        }
        ins.append(m)
    return ins


def kernel(L0_attn_x, L0_attn_Wq, L0_attn_bq, L0_attn_Wk, L0_attn_bk,
           L0_attn_Wv, L0_attn_bv, L0_attn_Wo, L0_attn_bo,
           L0_ffn_weight_int4, L0_ffn_weight_scale, L0_ffn_bias):
    cfg = CFG
    c = derive(cfg)
    ins = make_inputs(cfg, L0_attn_x, L0_attn_Wq, L0_attn_bq, L0_attn_Wk,
                      L0_attn_bk, L0_attn_Wv, L0_attn_bv, L0_attn_Wo,
                      L0_attn_bo, L0_ffn_weight_int4, L0_ffn_weight_scale,
                      L0_ffn_bias)
    nc = build(cfg)
    res = run_bass_kernel_spmd(nc, ins, core_ids=list(range(NCORES)))
    outT = np.concatenate([np.asarray(r["outT"]) for r in res.results], axis=0)
    out = np.ascontiguousarray(outT.T).reshape(c["B"], c["L"], c["D"])
    return out.astype(np.float32)
